# version 28
# baseline (speedup 1.0000x reference)
import sys
sys.path.insert(0, "/opt/trn_rl_repo")
import numpy as np
import ml_dtypes
import concourse.bass as bass
import concourse.mybir as mybir
from concourse.bass_utils import run_bass_kernel_spmd

F32 = mybir.dt.float32
BF16 = mybir.dt.bfloat16
U8 = mybir.dt.uint8
AF = mybir.ActivationFunctionType
ALU = mybir.AluOpType
AX = mybir.AxisListType

D = 4096
T = 4096
BATCH = 2
G = 8
HPG = 4
H = 32
HD = 128
NF = 32
OWN = 1024
NE = round(D * 0.8)
EI = -0.8 / (1.0 - 0.8)
EPS = 1e-5
ATT_SCALE = float(1.0 / (HD ** 0.5))
EXP_SHIFT = -20.0
CH = 1024
NCH = T // CH
INV_D = float(1.0 / D)

BF = ml_dtypes.bfloat16


class _Op:
    __slots__ = ("eng", "fn", "deps", "sig", "val", "idx", "chan", "fence")

    def __init__(self, eng, fn, deps, idx, chan, fence):
        self.eng = eng
        self.fn = fn
        self.deps = deps
        self.sig = False
        self.val = None
        self.idx = idx
        self.chan = chan
        self.fence = fence


class Prog:

    DMA_ENGS = ("q", "g")

    def __init__(self):
        self.ops = []

    def op(self, eng, fn, deps=(), chan=None, fence=False):
        pass
        o = _Op(eng, fn, [d for d in deps if d is not None], len(self.ops), chan, fence)
        for d in o.deps:
            assert d.idx < o.idx
        self.ops.append(o)
        return o

    def finalize(self):
        for o in self.ops:
            if o.chan is not None:
                o.sig = True
            for d in o.deps:
                if d.eng != o.eng or d.fence or o.chan is not None:
                    d.sig = True
        counts = {}
        for o in self.ops:
            if o.sig:
                key = o.chan if o.chan is not None else o.eng
                inc = 16 if o.chan is not None else 1
                counts[key] = counts.get(key, 0) + inc
                o.val = counts[key]
        self.totals = counts

    def run_engine(self, eng_name, eng, sems):
        last_wait = {}
        for o in self.ops:
            if o.eng != eng_name:
                continue
            for d in o.deps:
                key = d.chan if d.chan is not None else d.eng
                if (d.eng == eng_name and d.chan is None and not d.fence
                        and o.chan is None):
                    continue
                if d.val > last_wait.get(key, 0):
                    eng.wait_ge(sems[key], d.val)
                    last_wait[key] = d.val
            inst = o.fn(eng)
            if o.sig:
                key = o.chan if o.chan is not None else o.eng
                inst.then_inc(sems[key], 16 if o.chan is not None else 1)
        for o in self.ops:
            if o.eng == eng_name and o.chan is not None:
                t = self.totals[o.chan]
                if last_wait.get(o.chan, 0) < t:
                    eng.wait_ge(sems[o.chan], t)
                    last_wait[o.chan] = t


def build_program(dbg=()):
    nc = bass.Bass("TRN2", target_bir_lowering=False)
    pr = Prog()

    def dram(name, shape, dt, kind):
        return nc.dram_tensor(name, shape, dt, kind=kind).ap()

    def scratch(name, shape, dt):
        kind = "ExternalOutput" if name in dbg else "Internal"
        return dram(name, shape, dt, kind)

    x_kv = dram("x_kv", [2 * OWN, D], F32, "ExternalInput")
    wkv_c = dram("wkv_c", [D, 1024], F32, "ExternalInput")
    wq_t = dram("wq_t", [8, D, 512], F32, "ExternalInput")
    wo_t = dram("wo_t", [8, D, 512], F32, "ExternalInput")
    w0_t = dram("w0_t", [8, D, 512], U8, "ExternalInput")
    w1_t = dram("w1_t", [8, D, 512], U8, "ExternalInput")
    w2_t = dram("w2_t", [8, D, 512], U8, "ExternalInput")
    bo_t = dram("bo_t", [128, NF], F32, "ExternalInput")
    eisc_t = dram("eisc_t", [128, NF], F32, "ExternalInput")
    sA_t = dram("sA_t", [128, 4], F32, "ExternalInput")
    ident_t = dram("ident_t", [128, 128], BF16, "ExternalInput")
    msk4_t = dram("msk4_t", [4, 128, 128], BF16, "ExternalInput")
    out_t = dram("out", [OWN, D], F32, "ExternalOutput")

    xnkscr = scratch("xnkscr", [2 * OWN, D], BF16)
    k_src = scratch("k_src", [512, 2 * OWN], BF16)
    v_src = scratch("v_src", [2 * OWN, 512], BF16)
    k_dst = scratch("k_dst", [4 * 512, 2 * OWN], BF16)
    v_dst = scratch("v_dst", [4 * 2 * OWN, 512], BF16)
    qscr = scratch("qscr", [H, 128, OWN], BF16)
    gscr = scratch("gscr", [NF, 128, OWN], BF16)
    xnscr = scratch("xnscr", [NF, 128, OWN], BF16) if "xnscr" in dbg else None
    statscr = scratch("statscr", [8, 3, 128, 1], F32) if "statscr" in dbg else None
    ctxscr = scratch("ctxscr", [NF, 128, OWN], BF16) if "ctxscr" in dbg else None
    sgscr = scratch("sgscr", [NF, 128, OWN], BF16) if "sgscr" in dbg else None
    dscr = scratch("dscr", [H, 2, 1, 512], F32) if "dscr" in dbg else None
    pscr = scratch("pscr", [16, 128, 512], BF16) if "pscr" in dbg else None

    allocs = {}

    def salloc(name, shape, dt, side="left"):
        cm = nc.sbuf_tensor(name, shape, dt, side=side)
        t = cm.__enter__()
        allocs[name] = cm
        return t

    def sfree(*names):
        for n in names:
            allocs.pop(n).__exit__(None, None, None)

    ident = salloc("ident", [128, 128], BF16)

    bo_sb = salloc("bo_sb", [128, NF], F32)
    eisc_sb = salloc("eisc_sb", [128, NF], F32)
    sA = salloc("sA", [128, 4], F32)
    ones_c = salloc("ones_c", [128, 1], F32)
    ones_r = salloc("ones_r", [1, 128], F32)
    epsb = salloc("epsb", [128, 1], F32)
    eshift = salloc("eshift", [128, 1], F32)
    nsum = [salloc(f"nsum{i}", [128, 1], F32) for i in range(2)]
    nmean = [salloc(f"nmean{i}", [128, 1], F32) for i in range(2)]
    msq = [salloc(f"msq{i}", [128, 1], F32) for i in range(2)]
    varb = [salloc(f"varb{i}", [128, 1], F32) for i in range(2)]
    varo = [salloc(f"varo{i}", [128, 1], F32) for i in range(2)]
    sqv = [salloc(f"sqv{i}", [128, 1], F32) for i in range(2)]
    rstd = [salloc(f"rstd{i}", [128, 1], F32) for i in range(2)]

    psums = {}

    def palloc(name, shape, dt=F32):
        cm = nc.psum_tensor(name, shape, dt)
        t = cm.__enter__()
        psums[name] = cm
        return t

    def pfree(*names):
        for n in names:
            psums.pop(n).__exit__(None, None, None)

    psA = [palloc(f"psA{i}", [128, 512]) for i in range(8)]

    chan_names = (["t", "v", "s", "q", "g"]
                  + [f"c{i}" for i in range(5)]
                  + [f"x{i}" for i in range(2)] + [f"xr{i}" for i in range(3)]
                  + [f"xnw{i}" for i in range(2)] + [f"xt{i}" for i in range(4)]
                  + [f"xts{i}" for i in range(4)]
                  + ["wr0", "wr1"]
                  + [f"w{i}" for i in range(6)]
                  + [f"ks{i}" for i in range(4)] + [f"vs{i}" for i in range(4)]
                  + [f"qs{i}" for i in range(2)] + [f"gs{i}" for i in range(2)]
                  + [f"kb{i}" for i in range(2)] + [f"vb{i}" for i in range(2)]
                  + [f"qh{i}" for i in range(4)] + [f"gt{i}" for i in range(4)]
                  + [f"or{i}" for i in range(3)] + ["dbgx"])
    sem_cms = {k: nc.semaphore(f"sem_{k}") for k in chan_names}
    sems = {k: cm.__enter__() for k, cm in sem_cms.items()}

    bank_user = {}

    def consume_bank(b, op):
        bank_user[b] = op

    c_ident = pr.op("q", lambda eng: eng.dma_start(out=ident[:], in_=ident_t), chan="c0")
    c_msk = None
    c_bo = pr.op("q", lambda eng: eng.dma_start(out=bo_sb[:], in_=bo_t), chan="c2")
    c_eisc = pr.op("q", lambda eng: eng.dma_start(out=eisc_sb[:], in_=eisc_t), chan="c3")
    c_sA = pr.op("q", lambda eng: eng.dma_start(out=sA[:], in_=sA_t), chan="c4")
    o_ones = pr.op("v", lambda eng: eng.memset(ones_c[:], 1.0))
    o_onesr = pr.op("v", lambda eng: eng.memset(ones_r[:], 1.0))
    o_eps = pr.op("v", lambda eng: eng.memset(epsb[:], EPS))
    o_esh = pr.op("v", lambda eng: eng.memset(eshift[:], EXP_SHIFT))

    x_consumer = [None, None]
    xn_reader = [None, None]
    wtile_user = [None] * 6

    xn_writer = [None, None]
    xn_reader = [None, None]

    def ln_tile(src_fn, st, dst_scr, dst_row):
        sl = st % 2
        ldx = pr.op("q", lambda eng, sl=sl: eng.dma_start(out=x_sb[sl][:], in_=src_fn()),
                    deps=[x_consumer[sl]], chan=f"x{sl}")
        rs = pr.op("v", lambda eng, sl=sl: eng.reduce_sum(
            nsum[sl][:], x_sb[sl][:], axis=AX.X, negate=True), deps=[ldx], fence=True)
        sq = pr.op("s", lambda eng, sl=sl: eng.activation(
            xn_sb[0][:], x_sb[sl][:], AF.Square, accum_out=varb[sl][:]),
            deps=[ldx, xn_writer[0], xn_reader[0]])
        nm = pr.op("v", lambda eng, sl=sl: eng.tensor_scalar_mul(
            nmean[sl][:], nsum[sl][:], INV_D), deps=[rs], fence=True)
        mq = pr.op("s", lambda eng, sl=sl: eng.activation(
            msq[sl][:], nmean[sl][:], AF.Square), deps=[nm])
        vr = pr.op("v", lambda eng, sl=sl: eng.tensor_scalar(
            varo[sl][:], varb[sl][:], INV_D, msq[sl][:], ALU.mult, ALU.subtract),
            deps=[sq, mq])
        s2 = pr.op("s", lambda eng, sl=sl: eng.activation(
            sqv[sl][:], varo[sl][:], AF.Sqrt, bias=epsb[:], scale=1.0),
            deps=[vr, o_eps])
        rt = pr.op("v", lambda eng, sl=sl: eng.reciprocal(rstd[sl][:], sqv[sl][:]),
                   deps=[s2], fence=True)
        xn = pr.op("v", lambda eng, sl=sl: eng.tensor_scalar(
            xn_sb[0][:], x_sb[sl][:], nmean[sl][:], rstd[sl][:], ALU.add, ALU.mult),
            deps=[rt, sq])
        x_consumer[sl] = xn
        if dst_scr is None:
            return xn
        wr = pr.op("s", lambda eng, dst_row=dst_row: eng.dma_start(
            out=dst_scr[dst_row:dst_row + 128, :], in_=xn_sb[0][:]),
            deps=[xn], chan="xnw0")
        xn_writer[0] = wr
        return wr

    x_sb = [salloc(f"x_sb{i}", [128, D], F32, side="right") for i in range(2)]
    xn_sb = [salloc("xn_sb0", [128, D], BF16, side="right")]
    wkv_res = salloc("wkv_res", [128, NF * 1024], BF16, side="right")
    wkv32 = [salloc(f"wkv32_{i}", [128, 4096], F32) for i in range(2)]
    wkv_gate = []
    _wkv_cvt = [None, None]
    for blk in range(8):
        sl = blk % 2
        ld = pr.op("q", lambda eng, blk=blk, sl=sl: eng.dma_start(
            out=wkv32[sl][:].rearrange("p (a c) -> p a c", c=1024),
            in_=wkv_c[blk * 512:(blk + 1) * 512, :].rearrange(
                "(a p) c -> p a c", p=128)),
            deps=[_wkv_cvt[sl]], chan=f"wr{sl}")
        cv = pr.op("v", lambda eng, blk=blk, sl=sl: eng.tensor_copy(
            wkv_res[:, blk * 4096:(blk + 1) * 4096], wkv32[sl][:]), deps=[ld])
        _wkv_cvt[sl] = cv
        wkv_gate = [cv] if blk == 7 else wkv_gate + ([cv] if blk == 6 else [])
    sfree("wkv32_1", "wkv32_0")
    xnT_own = salloc("xnT_own", [128, 2 * NF * 512], BF16)
    xnT_tmp = salloc("xnT_tmp", [128, NF * 512], BF16)
    kst = [salloc(f"kst{i}", [128, 512], BF16) for i in range(2)]
    vfm = [salloc(f"vfm{i}", [128, 512], BF16, side="right") for i in range(2)]
    vst = [salloc(f"vst{i}", [128, 512], BF16, side="right") for i in range(2)]
    kst_user = [None] * 2
    vfm_user = [None] * 2
    vst_user = [None] * 2
    kv_write_ops = []

    def xnT_tile(ck, d):
        if ck < 2:
            return xnT_own[:, (ck * NF + d) * 512:(ck * NF + d + 1) * 512]
        return xnT_tmp[:, d * 512:(d + 1) * 512]

    xnT_view_own = xnT_own[:].rearrange("p (b f c) -> p b f c", b=2, c=512)
    xnT_view_tmp = xnT_tmp[:].rearrange("p (f c) -> p f c", c=512)

    def emit_ln(ci, st):
        base = ci * 512
        return ln_tile((lambda r0=base + st * 128: x_kv[r0:r0 + 128, :]), st, None, 0)

    region_user = [[None] * 4, [None] * 4, [None] * 4]

    def emit_tr_subtile(ck, st, xn, ready):
        rbuf = min(ck, 2)
        for f4 in range(8):
            b = 4 + f4 % 2
            mms = []
            for fi in range(4):
                f = f4 * 4 + fi
                dd = [xn, c_ident]
                if fi == 0:
                    dd.append(bank_user.get(b))
                mms.append(pr.op("t", lambda eng, f=f, fi=fi, b=b: eng.matmul(
                    psA[b][:, fi * 128:(fi + 1) * 128],
                    xn_sb[0][:, f * 128:(f + 1) * 128], ident[:],
                    start=True, stop=True), dd))
            if ck < 2:
                out_ap = xnT_view_own[:, ck, f4 * 4:(f4 + 1) * 4,
                                      st * 128:(st + 1) * 128]
            else:
                out_ap = xnT_view_tmp[:, f4 * 4:(f4 + 1) * 4, st * 128:(st + 1) * 128]
            ev = pr.op("v", lambda eng, b=b, out_ap=out_ap: eng.tensor_copy(
                out_ap, psA[b][:].rearrange("p (a c) -> p a c", c=128)),
                deps=mms + [region_user[rbuf][f4 // 2]])
            consume_bank(b, ev)
            for fi in range(4):
                ready[f4 * 4 + fi] = ev
        xn_reader[0] = mms[-1]

    ready_by_ck = [[None] * NF for _ in range(4)]
    for st in range(4):
        emit_tr_subtile(0, st, emit_ln(0, st), ready_by_ck[0])

    for ck in range(4):
        xnT_ready = ready_by_ck[ck]
        ready_next = ready_by_ck[ck + 1] if ck + 1 < 4 else None
        ln_next = [emit_ln(ck + 1, 0)] if ck + 1 < 3 else []
        seg = 0
        for w in range(2):
            lasts = [None] * 4
            for dblk in range(2):
                for d in range(dblk * 16, dblk * 16 + 16):
                    for gi in range(4):
                        stf = d == 0
                        spf = d == NF - 1
                        dd = wkv_gate + [xnT_ready[d]]
                        if stf:
                            dd.append(bank_user.get(gi))
                        lasts[gi] = pr.op("t", (lambda eng, gi=gi, w=w, d=d, ck=ck,
                                                stf=stf, spf=spf:
                                                eng.matmul(psA[gi][:],
                                                           wkv_res[:, d * 1024 + w * 512
                                                                   + gi * 128:
                                                                   d * 1024 + w * 512
                                                                   + (gi + 1) * 128],
                                                           xnT_tile(ck, d),
                                                           start=stf, stop=spf)), dd)
                    if w == 1 and d % 8 == 7:
                        region_user[min(ck, 2)][(d // 8) % 4] = lasts[3]
                if ck + 1 < 3 and seg < 4:
                    emit_tr_subtile(ck + 1, seg, ln_next[seg], ready_next)
                    if seg + 1 < 4:
                        ln_next.append(emit_ln(ck + 1, seg + 1))
                    seg += 1
            if w == 0:
                for s in range(4):
                    sb2 = s % 2
                    ev = pr.op("v", lambda eng, s=s, sb2=sb2: eng.tensor_copy(
                        kst[sb2][:], psA[s][:]), deps=[lasts[s], kst_user[sb2]])
                    consume_bank(s, ev)
                    wr = pr.op("s", lambda eng, s=s, sb2=sb2, ck=ck: eng.dma_start(
                        out=k_src[s * 128:(s + 1) * 128, ck * 512:(ck + 1) * 512],
                        in_=kst[sb2][:]), deps=[ev], chan=f"ks{s % 2}")
                    kst_user[sb2] = wr
                    kv_write_ops.append(wr)
            else:
                for s in range(4):
                    sb2 = s % 2
                    ev = pr.op("v", lambda eng, s=s, sb2=sb2: eng.tensor_copy(
                        vfm[sb2][:], psA[s][:]), deps=[lasts[s], vfm_user[sb2]])
                    consume_bank(s, ev)
                    b = 6 + s % 2
                    tmms = []
                    for fi in range(4):
                        dd = [ev, c_ident]
                        if fi == 0:
                            dd.append(bank_user.get(b))
                        tmms.append(pr.op("t", lambda eng, sb2=sb2, fi=fi, b=b: eng.matmul(
                            psA[b][:, fi * 128:(fi + 1) * 128],
                            vfm[sb2][:, fi * 128:(fi + 1) * 128], ident[:],
                            start=True, stop=True), dd))
                    cp = pr.op("s", lambda eng, sb2=sb2, b=b: eng.copy(
                        vst[sb2][:], psA[b][:]), deps=tmms + [vst_user[sb2]])
                    consume_bank(b, cp)
                    vfm_user[sb2] = tmms[-1]
                    wr = pr.op("s", lambda eng, s=s, sb2=sb2, ck=ck: eng.dma_start(
                        out=v_src[ck * 512:(ck + 1) * 512, s * 128:(s + 1) * 128]
                            .rearrange("(a p) c -> p a c", p=128),
                        in_=vst[sb2][:].rearrange("p (a c) -> p a c", c=128)),
                        deps=[cp], chan=f"vs{s % 2}")
                    vst_user[sb2] = wr
                    kv_write_ops.append(wr)

        if ck == 2:
            for st in range(4):
                xn3 = emit_ln(3, st)
                emit_tr_subtile(3, st, xn3, ready_by_ck[3])

    own_ready = [[ready_by_ck[0][d], ready_by_ck[1][d]] for d in range(NF)]
    kv_lasts = list({o.chan: o for o in kv_write_ops}.values())
    RG = [[0, 1, 2, 3], [4, 5, 6, 7]]
    ag_k = pr.op("g", lambda eng: eng.collective_compute(
        "AllGather", ALU.bypass, replica_groups=RG, ins=[k_src], outs=[k_dst]),
        deps=kv_lasts)
    ag_v = pr.op("g", lambda eng: eng.collective_compute(
        "AllGather", ALU.bypass, replica_groups=RG, ins=[v_src], outs=[v_dst]),
        deps=kv_lasts)

    sfree("kst1", "kst0", "xnT_tmp")
    sfree("vst1", "vst0", "vfm1", "vfm0", "wkv_res", "xn_sb0", "x_sb1", "x_sb0")

    wtile32 = [salloc(f"wt32_{i}", [128, 2048], F32, side="right") for i in range(4)]
    wtileu8 = [salloc(f"wtu8_{i}", [128, 2048], U8, side="right") for i in range(4)]
    pmt = [salloc(f"pmt{i}", [128, 2048], BF16, side="right") for i in range(4)]

    u8_user = [None] * 4
    wtile_user = [None] * 4

    def proj_sweep(w_dram, in_slice, in_ready, consumer, pm_u8=False, gate=None,
                   ld_gate=()):
        for w in range(8):
            lasts = [None] * 8
            for d in range(NF):
                s = (d // 4) % 4
                di = d % 4
                if di == 0:
                    raw = wtileu8[s] if pm_u8 else wtile32[s]
                    wt = pr.op("q", lambda eng, w=w, d=d, raw=raw, wd=w_dram: eng.dma_start(
                        out=raw[:].rearrange("p (a c) -> p a c", c=512),
                        in_=wd[w, d * 128:(d + 4) * 128, :].rearrange(
                            "(a p) c -> p a c", p=128)),
                        deps=[u8_user[s]] + list(ld_gate), chan=f"w{s}")
                    if pm_u8:
                        pm = pr.op("v", lambda eng, s=s, raw=raw: eng.tensor_scalar(
                            pmt[s][:], raw[:], 2.0, -1.0, ALU.mult, ALU.add),
                            deps=[wt, wtile_user[s]])
                    else:
                        pm = pr.op("v", lambda eng, s=s, raw=raw: eng.tensor_copy(
                            pmt[s][:], raw[:]), deps=[wt, wtile_user[s]])
                    u8_user[s] = pm
                op_src = pmt[s]
                dep0 = pm
                for fi in range(4):
                    for tt in range(2):
                        slot = fi * 2 + tt
                        stf = d == 0
                        spf = d == NF - 1
                        dd = [dep0] + list(in_ready[d])
                        if stf:
                            dd.append(bank_user.get(slot))
                            if gate is not None:
                                dd.append(gate)
                        rhs_ap = in_slice(d, tt)
                        lasts[slot] = pr.op("t", (lambda eng, slot=slot, src=op_src, di=di,
                                                  fi=fi, stf=stf, spf=spf, rhs=rhs_ap:
                                                  eng.matmul(psA[slot][:],
                                                             src[:, di * 512 + fi * 128:
                                                                 di * 512 + (fi + 1) * 128],
                                                             rhs, start=stf, stop=spf)), dd)
                if di == 3:
                    wtile_user[s] = lasts[7]
            for fi in range(4):
                for tt in range(2):
                    slot = fi * 2 + tt
                    cop = consumer(w * 4 + fi, tt, slot, lasts[slot])
                    consume_bank(slot, cop)

    qstage = [salloc(f"qstage{i}", [128, OWN], BF16, side="right") for i in range(2)]
    qstage_user = [None, None]
    qscr_writes = []
    q_evs = {}

    def q_consume(f, tt, slot, mmop):
        sl = f % 2
        ev = pr.op("v", lambda eng, sl=sl, tt=tt, slot=slot: eng.tensor_copy(
            qstage[sl][:, tt * 512:(tt + 1) * 512], psA[slot][:]),
            deps=[mmop, qstage_user[sl]] + kv_lasts)
        q_evs.setdefault(f, []).append(ev)
        if tt == 1:
            wr = pr.op("s", lambda eng, f=f, sl=sl: eng.dma_start(
                out=qscr[f], in_=qstage[sl][:]), deps=q_evs[f], chan=f"qs{sl}")
            qstage_user[sl] = wr
            qscr_writes.append(wr)
        return ev

    own_slice = lambda d, tt: xnT_own[:, (tt * NF + d) * 512:(tt * NF + d + 1) * 512]
    proj_sweep(wq_t, own_slice, own_ready, q_consume, ld_gate=kv_lasts)
    q_lasts = list({o.chan: o for o in qscr_writes}.values())

    relu_sb = [salloc(f"relu{i}", [128, 512], F32, side="right") for i in range(4)]
    relu_user = [None] * 4
    h_a = [salloc(f"h_a{f}", [128, OWN], BF16, side="right") for f in range(NF)]
    h_ready = [None] * NF

    def h_ready_l():
        return [[h_ready[d]] for d in range(NF)]

    def h2_ready_l():
        return [[h2_ready[d]] for d in range(NF)]

    def ffn_consume_factory(out_tiles, out_ready, s_col):
        def consume(f, tt, slot, mmop):
            rsl = slot % 4
            rl = pr.op("s", lambda eng, slot=slot, rsl=rsl: eng.activation(
                relu_sb[rsl][:], psA[slot][:], AF.Relu, scale=sA[:, s_col:s_col + 1]),
                deps=[mmop, relu_user[rsl], c_sA] + kv_lasts)
            mu = pr.op("v", lambda eng, f=f, tt=tt, rsl=rsl: eng.tensor_scalar_mul(
                out_tiles[f][:, tt * 512:(tt + 1) * 512], relu_sb[rsl][:],
                eisc_sb[:, f:f + 1]), deps=[rl, c_eisc])
            relu_user[rsl] = mu
            out_ready[f] = mu
            return rl
        return consume

    proj_sweep(w0_t, own_slice, own_ready, ffn_consume_factory(h_a, h_ready, 0), pm_u8=True)
    sfree("xnT_own")
    h_b = [salloc(f"h_b{f}", [128, OWN], BF16) for f in range(NF)]
    h2_ready = [None] * NF
    proj_sweep(w1_t, lambda d, tt: h_a[d][:, tt * 512:(tt + 1) * 512],
               [[h_ready[d]] for d in range(NF)] if False else h_ready_l(),
               ffn_consume_factory(h_b, h2_ready, 1), pm_u8=True)
    sfree(*(f"h_a{f}" for f in reversed(range(NF))))

    gstage = [salloc(f"gstage{i}", [128, OWN], BF16, side="right") for i in range(2)]
    gstage_user = [None, None]
    gscr_writes = []
    g_evs = {}

    def g_consume(f, tt, slot, mmop):
        sl = f % 2
        ev = pr.op("s", lambda eng, sl=sl, tt=tt, slot=slot: eng.activation(
            gstage[sl][:, tt * 512:(tt + 1) * 512], psA[slot][:], AF.Silu,
            scale=sA[:, 2:3]), deps=[mmop, gstage_user[sl], c_sA])
        g_evs.setdefault(f, []).append(ev)
        if tt == 1:
            wr = pr.op("s", lambda eng, f=f, sl=sl: eng.dma_start(
                out=gscr[f], in_=gstage[sl][:]), deps=g_evs[f], chan=f"gs{sl}")
            gstage_user[sl] = wr
            gscr_writes.append(wr)
        return ev

    proj_sweep(w2_t, lambda d, tt: h_b[d][:, tt * 512:(tt + 1) * 512],
               h2_ready_l(), g_consume, pm_u8=True)
    g_lasts = list({o.chan: o for o in gscr_writes}.values())
    sfree(*(f"h_b{f}" for f in reversed(range(NF))))
    sfree("gstage1", "gstage0",
          *(f"relu{i}" for i in reversed(range(4))),

          "qstage1", "qstage0")

    pfree(*(f"psA{i}" for i in reversed(range(8))))
    psSC = [palloc(f"psSC{i}", [128, 1024]) for i in range(2)]
    psCTX = [palloc(f"psCTX{i}", [128, 512]) for i in range(2)]
    psDEN = palloc("psDEN", [128, 512])
    psBC = palloc("psBC", [128, 512])

    ctx_sb = [salloc(f"ctx{f}", [128, OWN], BF16) for f in range(NF)]
    kbuf = [salloc(f"kbuf{i}", [128, T], BF16, side="right") for i in range(2)]
    vbuf = [salloc(f"vbuf{i}", [128, T], BF16, side="right") for i in range(2)]
    qhead = [salloc(f"qh{i}", [128, 512], BF16, side="right") for i in range(4)]
    probs = [salloc(f"probs{i}", [128, 1024], BF16, side="right") for i in range(3)]
    bc_sb = [salloc(f"bc_sb{i}", [128, 512], F32, side="right") for i in range(2)]
    pacc = [salloc(f"pacc{i}", [128, 512], BF16, side="right") for i in range(2)]
    rrow = [salloc(f"rrow{i}", [1, 512], F32, side="right") for i in range(2)]
    ones_cb = salloc("ones_cb", [128, 1], BF16, side="right")
    o_onescb = pr.op("v", lambda eng: eng.memset(ones_cb[:], 1.0))
    msk4 = salloc("msk4", [128, 4 * 128], BF16, side="right")
    c_msk = pr.op("q", lambda eng: eng.dma_start(
        out=msk4[:].rearrange("p (a c) -> p a c", c=128),
        in_=msk4_t.rearrange("a p c -> p a c")), chan="c1")

    kv_user = [None, None]
    qhead_user = [None] * 4
    probs_user = [[], [], []]
    pacc_user = [None, None]
    ctx_ready = [None] * NF
    pending_tail = None

    for qt in range(2):
        nkc = (qt + 1) * 16
        ng = nkc // 2
        for gg in range(G):
            sl = gg % 2
            gl = gg // 2
            b_in = gg % 2
            E4 = (qt + 1) * 512
            ldk_ops = []
            ldv_ops = []
            for r4 in range(4):
                ldk_ops.append(pr.op("q", lambda eng, r4=r4, gl=gl, b_in=b_in, sl=sl,
                                     E4=E4: eng.dma_start(
                    out=kbuf[sl][:, r4 * E4:(r4 + 1) * E4],
                    in_=k_dst[r4 * 512 + gl * 128:r4 * 512 + (gl + 1) * 128,
                              b_in * OWN:b_in * OWN + E4]),
                    deps=[ag_k] + g_lasts + [kv_user[sl]], chan=f"kb{sl}"))
                ldv_ops.append(pr.op("q", lambda eng, r4=r4, gl=gl, b_in=b_in, sl=sl,
                                     E4=E4: eng.dma_start(
                    out=vbuf[sl][:, r4 * E4:(r4 + 1) * E4].rearrange(
                        "p (a c) -> p a c", c=128),
                    in_=v_dst[r4 * 2 * OWN + b_in * OWN:
                              r4 * 2 * OWN + b_in * OWN + E4, gl * 128:(gl + 1) * 128]
                        .rearrange("(a p) c -> p a c", p=128)),
                    deps=[ag_v] + g_lasts + [kv_user[sl]], chan=f"vb{sl}"))
            ldk = ldk_ops[-1]
            ldv = ldv_ops[-1]
            for h in range(HPG):
                i = gg * HPG + h
                qsl = i % 4
                ldq = pr.op("q", lambda eng, i=i, qt=qt, qsl=qsl: eng.dma_start(
                    out=qhead[qsl][:], in_=qscr[i][:, qt * 512:(qt + 1) * 512]),
                    deps=q_lasts + g_lasts + [qhead_user[qsl]], chan=f"qh{qsl}")
                ctx_bank = 4 + h % 2

                def emit_scores(kg, ldk=ldk, ldq=ldq, sl=sl, qsl=qsl):
                    sb = kg % 2
                    mms = []
                    for ci in range(2):
                        kc = kg * 2 + ci
                        bank = sb * 2 + ci
                        mms.append(pr.op("t", lambda eng, sb=sb, ci=ci, kc=kc, sl=sl,
                                         qsl=qsl: eng.matmul(
                            psSC[sb][:, ci * 512:(ci + 1) * 512],
                            kbuf[sl][:, kc * 128:(kc + 1) * 128], qhead[qsl][:],
                            start=True, stop=True),
                            deps=[ldk, ldq, bank_user.get(bank)]))
                    return mms

                sc_q = [emit_scores(0)]
                if ng > 1:
                    sc_q.append(emit_scores(1))
                last_den = None
                last_ctx = None
                for kg in range(ng):
                    mms = sc_q.pop(0)
                    if kg == 0 and pending_tail is not None:
                        pending_tail()
                        pending_tail = None
                    sb = kg % 2
                    psl = kg % 3
                    ex = pr.op("s", lambda eng, sb=sb, psl=psl: eng.activation(
                        probs[psl][:], psSC[sb][:], AF.Exp, bias=eshift[:],
                        scale=ATT_SCALE), deps=mms + [o_esh] + probs_user[psl])
                    consume_bank(sb * 2, ex)
                    consume_bank(sb * 2 + 1, ex)
                    if kg + 2 < ng:
                        sc_q.append(emit_scores(kg + 2))
                    vops_ci = [[], []]
                    npr = (qt + 1) * 4
                    for ci in range(2):
                        kc = kg * 2 + ci
                        r4 = kc // npr
                        lp = kc % npr
                        dband = lp - qt * 4
                        cb = ci * 512
                        if dband >= 0:
                            if dband > 0:
                                vops_ci[ci].append(pr.op("v", lambda eng, psl=psl, cb=cb,
                                                         dband=dband: eng.memset(
                                    probs[psl][:, cb:cb + 128 * dband], 0.0), deps=[ex]))
                            vops_ci[ci].append(pr.op("v", lambda eng, psl=psl, cb=cb,
                                                     dband=dband, r4=r4: eng.tensor_tensor(
                                probs[psl][:, cb + 128 * dband:cb + 128 * dband + 128],
                                probs[psl][:, cb + 128 * dband:cb + 128 * dband + 128],
                                msk4[:, r4 * 128:(r4 + 1) * 128], ALU.mult),
                                deps=[ex, c_msk]))
                    pp = pr.op("v", lambda eng, psl=psl, kg=kg: eng.tensor_tensor(
                        pacc[kg % 2][:], probs[psl][:, 0:512], probs[psl][:, 512:1024],
                        ALU.add), deps=[ex] + vops_ci[0] + vops_ci[1] +
                        [pacc_user[kg % 2]], fence=True)
                    pacc_user[kg % 2] = None
                    dd = [pp, o_onescb]
                    if kg == 0:
                        dd.append(bank_user.get(6))
                    last_den = pr.op("t", (lambda eng, kg=kg, stf=(kg == 0),
                                           spf=(kg == ng - 1):
                                           eng.matmul(psDEN[0:1, :], ones_cb[:],
                                                      pacc[kg % 2][:],
                                                      start=stf, stop=spf)), dd)
                    pacc_user[kg % 2] = last_den
                    for ci in range(2):
                        kc = kg * 2 + ci
                        cb = ci * 512
                        dd = [ldv, ex] + vops_ci[ci]
                        if kc == 0:
                            dd.append(bank_user.get(ctx_bank))
                        last_ctx = pr.op("t", (lambda eng, psl=psl, cb=cb, kc=kc, sl=sl,
                                               cbk=ctx_bank, stf=(kc == 0),
                                               spf=(kc == nkc - 1):
                                               eng.matmul(psCTX[cbk - 4][:],
                                                          vbuf[sl][:, kc * 128:(kc + 1) * 128],
                                                          probs[psl][:, cb:cb + 512],
                                                          start=stf, stop=spf)), dd)
                    probs_user[psl] = [last_ctx]
                qhead_user[qsl] = last_ctx

                def tail(i=i, qt=qt, cbk=ctx_bank, last_den=last_den,
                         last_ctx=last_ctx):
                    par = (i * 2 + qt) % 2
                    rc = pr.op("v", lambda eng, par=par: eng.reciprocal(
                        rrow[par][:], psDEN[0:1, :]), deps=[last_den], fence=True)
                    consume_bank(6, rc)
                    bc_mm = pr.op("t", lambda eng, par=par: eng.matmul(
                        psBC[:], ones_r[:], rrow[par][:], start=True, stop=True),
                        deps=[rc, o_onesr, bank_user.get(7)])
                    bcc = pr.op("v", lambda eng, par=par: eng.tensor_copy(
                        bc_sb[par][:], psBC[:]), deps=[bc_mm], fence=True)
                    consume_bank(7, bcc)
                    nrm = pr.op("v", lambda eng, i=i, qt=qt, par=par, cbk=cbk:
                                eng.tensor_tensor(
                        ctx_sb[i][:, qt * 512:(qt + 1) * 512], psCTX[cbk - 4][:],
                        bc_sb[par][:], ALU.mult), deps=[last_ctx, bcc])
                    consume_bank(cbk, nrm)
                    ctx_ready[i] = [nrm]
                    return nrm

                pending_tail = tail
            kv_user[sl] = qhead_user[(gg * HPG + HPG - 1) % 4]

    if pending_tail is not None:
        pending_tail()
        pending_tail = None
    d_done = ctx_ready[H - 1][0]
    if ctxscr is not None:
        for f in range(NF):
            pr.op("q", lambda eng, f=f: eng.dma_start(out=ctxscr[f], in_=ctx_sb[f][:]),
                  deps=ctx_ready[f], chan="dbgx")
    sfree("msk4", "ones_cb", "rrow1", "rrow0", "pacc1", "pacc0", "bc_sb1", "bc_sb0",
          "probs2", "probs1", "probs0",
          "qh3", "qh2", "qh1", "qh0", "vbuf1", "vbuf0", "kbuf1", "kbuf0")
    pfree("psBC", "psDEN", "psCTX1", "psCTX0", "psSC1", "psSC0")
    psA = [palloc(f"psE{i}", [128, 512]) for i in range(8)]

    sg = [salloc(f"sg{f}", [128, OWN], BF16, side="right") for f in range(NF)]
    s_sb = [salloc(f"s_sb{i}", [128, 512], BF16, side="right") for i in range(2)]
    gtile = [salloc(f"gtile{i}", [128, OWN], BF16, side="right") for i in range(4)]
    s_user = [None, None]
    gtile_user = [None] * 4
    gld = {}
    sg_ready = [None] * NF

    def prefetch_g(f):
        if f not in gld:
            gld[f] = pr.op("q", lambda eng, f=f: eng.dma_start(
                out=gtile[f % 4][:], in_=gscr[f]),
                deps=g_lasts + [d_done, gtile_user[f % 4]], chan=f"gt{f % 4}")
        return gld[f]

    def wo_consume(f, tt, slot, mmop):
        ssl = slot % 2
        ev = pr.op("s", lambda eng, slot=slot, ssl=ssl, f=f: eng.activation(
            s_sb[ssl][:], psA[slot][:], AF.Identity, bias=bo_sb[:, f:f + 1], scale=1.0),
            deps=[mmop, s_user[ssl], c_bo])
        gop = prefetch_g(f)
        mu = pr.op("v", lambda eng, f=f, tt=tt, ssl=ssl: eng.tensor_tensor(
            sg[f][:, tt * 512:(tt + 1) * 512], s_sb[ssl][:],
            gtile[f % 4][:, tt * 512:(tt + 1) * 512], ALU.mult), deps=[ev, gop])
        s_user[ssl] = mu
        gtile_user[f % 4] = mu
        sg_ready[f] = mu
        return ev

    proj_sweep(wo_t, lambda d, tt: ctx_sb[d][:, tt * 512:(tt + 1) * 512],
               ctx_ready, wo_consume, gate=d_done)
    sfree("gtile3", "gtile2", "gtile1", "gtile0", "s_sb1", "s_sb0")
    sfree(*(f"ctx{f}" for f in reversed(range(NF))))

    if sgscr is not None:
        for f in range(NF):
            pr.op("q", lambda eng, f=f: eng.dma_start(out=sgscr[f], in_=sg[f][:]),
                  deps=[sg_ready[f]], chan="dbgx")
    xrow = [salloc(f"xrow{i}", [128, D], F32) for i in range(3)]
    orow = [salloc(f"orow{i}", [128, D], F32) for i in range(2)]
    xrow_user = [None] * 3
    orow_user = [None] * 2
    F_BANKS = (4, 5)
    e_gate = sg_ready[NF - 1]
    for ts in range(8):
        sl = ts % 3
        so = ts % 2
        ldx = pr.op("q", lambda eng, ts=ts, sl=sl: eng.dma_start(
            out=xrow[sl][:], in_=x_kv[ts * 128:(ts + 1) * 128, :]),
            deps=[xrow_user[sl], e_gate], chan=f"xr{sl}")
        adds = []
        for f4 in range(8):
            b = F_BANKS[f4 % 2]
            mms = []
            for fi in range(4):
                f = f4 * 4 + fi
                dd = [sg_ready[f], c_ident]
                if fi == 0:
                    dd.append(bank_user.get(b))
                mms.append(pr.op("t", lambda eng, b=b, f=f, fi=fi, ts=ts: eng.matmul(
                    psA[b][:, fi * 128:(fi + 1) * 128], sg[f][:, ts * 128:(ts + 1) * 128],
                    ident[:], start=True, stop=True), dd))
            ad = pr.op("v", lambda eng, b=b, f4=f4, sl=sl, so=so: eng.tensor_tensor(
                orow[so][:, f4 * 512:(f4 + 1) * 512], xrow[sl][:, f4 * 512:(f4 + 1) * 512],
                psA[b][:], ALU.add), deps=mms + [ldx, orow_user[so]])
            consume_bank(b, ad)
            adds.append(ad)
        xrow_user[sl] = adds[-1]
        wr = pr.op("q", lambda eng, ts=ts, so=so: eng.dma_start(
            out=out_t[ts * 128:(ts + 1) * 128, :], in_=orow[so][:]), deps=adds,
            chan=f"or{so}")
        orow_user[so] = wr

    pr.finalize()
    with nc.Block() as block:
        @block.tensor
        def _(eng):
            pr.run_engine("t", eng, sems)

        @block.vector
        def _(eng):
            pr.run_engine("v", eng, sems)

        @block.scalar
        def _(eng):
            pr.run_engine("s", eng, sems)

        @block.sync
        def _(eng):
            pr.run_engine("q", eng, sems)

        @block.gpsimd
        def _(eng):
            pr.run_engine("g", eng, sems)

    for name in reversed(list(allocs)):
        sfree(name)
    for cm in reversed(list(psums.values())):
        cm.__exit__(None, None, None)
    for cm in reversed(list(sem_cms.values())):
        cm.__exit__(None, None, None)
    return nc


def _host_inputs(L0_attn_x, L0_attn_Wq, L0_attn_Wkv, L0_attn_Wo, L0_attn_bo,
                 L0_ffn_k0, L0_ffn_s0, L0_ffn_k1, L0_ffn_s1, L0_ffn_k2, L0_ffn_s2):
    x = np.ascontiguousarray(np.asarray(L0_attn_x, dtype=np.float32))
    Wq = np.asarray(L0_attn_Wq, dtype=np.float32)
    Wkv = np.asarray(L0_attn_Wkv, dtype=np.float32)
    Wo = np.asarray(L0_attn_Wo, dtype=np.float32)
    bo = np.asarray(L0_attn_bo, dtype=np.float32)

    kv4 = Wkv.reshape(D, G, 2, HD)
    wkv_by_b = []
    for b in range(BATCH):
        kc = kv4[:, 4 * b:4 * b + 4, 0, :].reshape(D, 512)
        vc = kv4[:, 4 * b:4 * b + 4, 1, :].reshape(D, 512)
        wkv_by_b.append(np.ascontiguousarray(np.concatenate([kc, vc], axis=1)))
    wq_t = np.ascontiguousarray(Wq.reshape(D, 8, 512).transpose(1, 0, 2))
    wo_t = np.ascontiguousarray(Wo.reshape(D, 8, 512).transpose(1, 0, 2))

    def tile_u8(k):
        a = np.asarray(k)
        a = a.view(np.uint8) if a.dtype == np.bool_ else a.astype(np.uint8)
        return np.ascontiguousarray(a.reshape(D, 8, 512).transpose(1, 0, 2))

    w0_t = tile_u8(L0_ffn_k0)
    w1_t = tile_u8(L0_ffn_k1)
    w2_t = tile_u8(L0_ffn_k2)

    bo_t = np.ascontiguousarray(bo.reshape(NF, 128).T)
    eisc = np.where(np.arange(D) < NE, np.float32(2.0), np.float32(2.0 * EI)).astype(np.float32)
    eisc_t = np.ascontiguousarray(eisc.reshape(NF, 128).T)
    sA_col = np.stack([np.float32(np.asarray(L0_ffn_s0)),
                       np.float32(np.asarray(L0_ffn_s1)),
                       np.float32(np.asarray(L0_ffn_s2)),
                       np.float32(0.0)])
    sA_t = np.ascontiguousarray(np.broadcast_to(sA_col[None, :], (128, 4))).astype(np.float32)
    ident_t = np.eye(128, dtype=BF)

    shared = dict(w0_t=w0_t, w1_t=w1_t,
                  w2_t=w2_t, bo_t=bo_t, eisc_t=eisc_t, sA_t=sA_t, ident_t=ident_t)
    perm = [1, 0, 3, 2, 5, 4, 7, 6]
    wq_t_b = [wq_t, np.ascontiguousarray(wq_t[perm])]
    Wo_p = np.ascontiguousarray(Wo.reshape(8, 512, D)[perm].reshape(D, D))
    wo_t_b = [wo_t, np.ascontiguousarray(Wo_p.reshape(D, 8, 512).transpose(1, 0, 2))]
    in_maps = []
    ki = np.arange(128)[:, None]
    wi = np.arange(128)[None, :]
    for c in range(8):
        b, j = divmod(c, 4)
        msk4 = np.stack([(ki <= wi - (1 if r4 > j else 0)) for r4 in range(4)])
        m = dict(shared)
        m["x_kv"] = np.ascontiguousarray(
            np.concatenate([x[b, j::4, :], x[1 - b, j::4, :]], axis=0))
        m["msk4_t"] = msk4.astype(BF)
        m["wkv_c"] = wkv_by_b[b]
        m["wq_t"] = wq_t_b[b]
        m["wo_t"] = wo_t_b[b]
        in_maps.append(m)
    return x, in_maps

def kernel(L0_attn_x, L0_attn_Wq, L0_attn_Wkv, L0_attn_Wo, L0_attn_bo,
           L0_ffn_k0, L0_ffn_s0, L0_ffn_k1, L0_ffn_s1, L0_ffn_k2, L0_ffn_s2):
    x, in_maps = _host_inputs(L0_attn_x, L0_attn_Wq, L0_attn_Wkv, L0_attn_Wo,
                              L0_attn_bo, L0_ffn_k0, L0_ffn_s0, L0_ffn_k1,
                              L0_ffn_s1, L0_ffn_k2, L0_ffn_s2)
    nc = build_program()
    res = run_bass_kernel_spmd(nc, in_maps, core_ids=list(range(8)))
    out = np.empty((BATCH, T, D), dtype=np.float32)
    for c in range(8):
        b, j = divmod(c, 4)
        out[b, j::4, :] = res.results[c]["out"]
    return out


# version 29 (final)
# speedup vs baseline: 1.0369x; 1.0369x over previous
import sys
sys.path.insert(0, "/opt/trn_rl_repo")
import numpy as np
import ml_dtypes
import concourse.bass as bass
import concourse.mybir as mybir
from concourse.bass_utils import run_bass_kernel_spmd

F32 = mybir.dt.float32
BF16 = mybir.dt.bfloat16
U8 = mybir.dt.uint8
AF = mybir.ActivationFunctionType
ALU = mybir.AluOpType
AX = mybir.AxisListType

D = 4096
T = 4096
BATCH = 2
G = 8
HPG = 4
H = 32
HD = 128
NF = 32
OWN = 1024
NE = round(D * 0.8)
EI = -0.8 / (1.0 - 0.8)
EPS = 1e-5
ATT_SCALE = float(1.0 / (HD ** 0.5))
EXP_SHIFT = -20.0
CH = 1024
NCH = T // CH
INV_D = float(1.0 / D)

BF = ml_dtypes.bfloat16


class _Op:
    __slots__ = ("eng", "fn", "deps", "sig", "val", "idx", "chan", "fence")

    def __init__(self, eng, fn, deps, idx, chan, fence):
        self.eng = eng
        self.fn = fn
        self.deps = deps
        self.sig = False
        self.val = None
        self.idx = idx
        self.chan = chan
        self.fence = fence


class Prog:

    DMA_ENGS = ("q", "g")

    def __init__(self):
        self.ops = []

    def op(self, eng, fn, deps=(), chan=None, fence=False):
        pass
        o = _Op(eng, fn, [d for d in deps if d is not None], len(self.ops), chan, fence)
        for d in o.deps:
            assert d.idx < o.idx
        self.ops.append(o)
        return o

    def finalize(self):
        for o in self.ops:
            if o.chan is not None:
                o.sig = True
            for d in o.deps:
                if d.eng != o.eng or d.fence or o.chan is not None:
                    d.sig = True
        counts = {}
        for o in self.ops:
            if o.sig:
                key = o.chan if o.chan is not None else o.eng
                inc = 16 if o.chan is not None else 1
                counts[key] = counts.get(key, 0) + inc
                o.val = counts[key]
        self.totals = counts

    def run_engine(self, eng_name, eng, sems):
        last_wait = {}
        for o in self.ops:
            if o.eng != eng_name:
                continue
            for d in o.deps:
                key = d.chan if d.chan is not None else d.eng
                if (d.eng == eng_name and d.chan is None and not d.fence
                        and o.chan is None):
                    continue
                if d.val > last_wait.get(key, 0):
                    eng.wait_ge(sems[key], d.val)
                    last_wait[key] = d.val
            inst = o.fn(eng)
            if o.sig:
                key = o.chan if o.chan is not None else o.eng
                inst.then_inc(sems[key], 16 if o.chan is not None else 1)
        for o in self.ops:
            if o.eng == eng_name and o.chan is not None:
                t = self.totals[o.chan]
                if last_wait.get(o.chan, 0) < t:
                    eng.wait_ge(sems[o.chan], t)
                    last_wait[o.chan] = t


def build_program(dbg=()):
    nc = bass.Bass("TRN2", target_bir_lowering=False)
    pr = Prog()

    def dram(name, shape, dt, kind):
        return nc.dram_tensor(name, shape, dt, kind=kind).ap()

    def scratch(name, shape, dt):
        kind = "ExternalOutput" if name in dbg else "Internal"
        return dram(name, shape, dt, kind)

    x_kv = dram("x_kv", [2 * OWN, D], F32, "ExternalInput")
    wkv_c = dram("wkv_c", [D, 1024], F32, "ExternalInput")
    wq_t = dram("wq_t", [8, D, 512], F32, "ExternalInput")
    wo_t = dram("wo_t", [8, D, 512], F32, "ExternalInput")
    w0_t = dram("w0_t", [8, D, 512], U8, "ExternalInput")
    w1_t = dram("w1_t", [8, D, 512], U8, "ExternalInput")
    w2_t = dram("w2_t", [8, D, 512], U8, "ExternalInput")
    bo_t = dram("bo_t", [128, NF], F32, "ExternalInput")
    eisc_t = dram("eisc_t", [128, NF], F32, "ExternalInput")
    sA_t = dram("sA_t", [128, 4], F32, "ExternalInput")
    ident_t = dram("ident_t", [128, 128], BF16, "ExternalInput")
    msk4_t = dram("msk4_t", [4, 128, 128], BF16, "ExternalInput")
    out_t = dram("out", [OWN, D], F32, "ExternalOutput")

    xnkscr = scratch("xnkscr", [2 * OWN, D], BF16)
    k_src = scratch("k_src", [512, 2 * OWN], BF16)
    v_src = scratch("v_src", [2 * OWN, 512], BF16)
    k_dst = scratch("k_dst", [4 * 512, 2 * OWN], BF16)
    v_dst = scratch("v_dst", [4 * 2 * OWN, 512], BF16)
    qscr = scratch("qscr", [H, 128, OWN], BF16)
    gscr = scratch("gscr", [NF, 128, OWN], BF16)
    xnscr = scratch("xnscr", [NF, 128, OWN], BF16) if "xnscr" in dbg else None
    statscr = scratch("statscr", [8, 3, 128, 1], F32) if "statscr" in dbg else None
    ctxscr = scratch("ctxscr", [NF, 128, OWN], BF16) if "ctxscr" in dbg else None
    sgscr = scratch("sgscr", [NF, 128, OWN], BF16) if "sgscr" in dbg else None
    dscr = scratch("dscr", [H, 2, 1, 512], F32) if "dscr" in dbg else None
    pscr = scratch("pscr", [16, 128, 512], BF16) if "pscr" in dbg else None

    allocs = {}

    def salloc(name, shape, dt, side="left"):
        cm = nc.sbuf_tensor(name, shape, dt, side=side)
        t = cm.__enter__()
        allocs[name] = cm
        return t

    def sfree(*names):
        for n in names:
            allocs.pop(n).__exit__(None, None, None)

    ident = salloc("ident", [128, 128], BF16)

    bo_sb = salloc("bo_sb", [128, NF], F32)
    eisc_sb = salloc("eisc_sb", [128, NF], F32)
    sA = salloc("sA", [128, 4], F32)
    ones_c = salloc("ones_c", [128, 1], F32)
    ones_r = salloc("ones_r", [1, 128], F32)
    epsb = salloc("epsb", [128, 1], F32)
    eshift = salloc("eshift", [128, 1], F32)
    nsum = [salloc(f"nsum{i}", [128, 1], F32) for i in range(2)]
    nmean = [salloc(f"nmean{i}", [128, 1], F32) for i in range(2)]
    msq = [salloc(f"msq{i}", [128, 1], F32) for i in range(2)]
    varb = [salloc(f"varb{i}", [128, 1], F32) for i in range(2)]
    varo = [salloc(f"varo{i}", [128, 1], F32) for i in range(2)]
    sqv = [salloc(f"sqv{i}", [128, 1], F32) for i in range(2)]
    rstd = [salloc(f"rstd{i}", [128, 1], F32) for i in range(2)]

    psums = {}

    def palloc(name, shape, dt=F32):
        cm = nc.psum_tensor(name, shape, dt)
        t = cm.__enter__()
        psums[name] = cm
        return t

    def pfree(*names):
        for n in names:
            psums.pop(n).__exit__(None, None, None)

    psA = [palloc(f"psA{i}", [128, 512]) for i in range(8)]

    chan_names = (["t", "v", "s", "q", "g"]
                  + [f"c{i}" for i in range(5)]
                  + [f"x{i}" for i in range(2)] + [f"xr{i}" for i in range(3)]
                  + [f"xnw{i}" for i in range(2)] + [f"xt{i}" for i in range(4)]
                  + [f"xts{i}" for i in range(4)]
                  + ["wr0", "wr1"]
                  + [f"w{i}" for i in range(6)]
                  + [f"ks{i}" for i in range(4)] + [f"vs{i}" for i in range(4)]
                  + [f"qs{i}" for i in range(2)] + [f"gs{i}" for i in range(2)]
                  + [f"kb{i}" for i in range(2)] + [f"vb{i}" for i in range(2)]
                  + [f"qh{i}" for i in range(4)] + [f"gt{i}" for i in range(4)]
                  + [f"or{i}" for i in range(3)] + ["dbgx"])
    sem_cms = {k: nc.semaphore(f"sem_{k}") for k in chan_names}
    sems = {k: cm.__enter__() for k, cm in sem_cms.items()}

    bank_user = {}

    def consume_bank(b, op):
        bank_user[b] = op

    c_ident = pr.op("q", lambda eng: eng.dma_start(out=ident[:], in_=ident_t), chan="c0")
    c_msk = None
    c_bo = pr.op("q", lambda eng: eng.dma_start(out=bo_sb[:], in_=bo_t), chan="c2")
    c_eisc = pr.op("q", lambda eng: eng.dma_start(out=eisc_sb[:], in_=eisc_t), chan="c3")
    c_sA = pr.op("q", lambda eng: eng.dma_start(out=sA[:], in_=sA_t), chan="c4")
    o_ones = pr.op("v", lambda eng: eng.memset(ones_c[:], 1.0))
    o_onesr = pr.op("v", lambda eng: eng.memset(ones_r[:], 1.0))
    o_eps = pr.op("v", lambda eng: eng.memset(epsb[:], EPS))
    o_esh = pr.op("v", lambda eng: eng.memset(eshift[:], EXP_SHIFT))

    x_consumer = [None, None]
    xn_reader = [None, None]
    wtile_user = [None] * 6

    xn_writer = [None, None]
    xn_reader = [None, None]

    def ln_tile(src_fn, st, dst_scr, dst_row):
        sl = st % 2
        ldx = pr.op("q", lambda eng, sl=sl: eng.dma_start(out=x_sb[sl][:], in_=src_fn()),
                    deps=[x_consumer[sl]], chan=f"x{sl}")
        rs = pr.op("v", lambda eng, sl=sl: eng.reduce_sum(
            nsum[sl][:], x_sb[sl][:], axis=AX.X, negate=True), deps=[ldx], fence=True)
        sq = pr.op("s", lambda eng, sl=sl: eng.activation(
            xn_sb[0][:], x_sb[sl][:], AF.Square, accum_out=varb[sl][:]),
            deps=[ldx, xn_writer[0], xn_reader[0]])
        nm = pr.op("v", lambda eng, sl=sl: eng.tensor_scalar_mul(
            nmean[sl][:], nsum[sl][:], INV_D), deps=[rs], fence=True)
        mq = pr.op("s", lambda eng, sl=sl: eng.activation(
            msq[sl][:], nmean[sl][:], AF.Square), deps=[nm])
        vr = pr.op("v", lambda eng, sl=sl: eng.tensor_scalar(
            varo[sl][:], varb[sl][:], INV_D, msq[sl][:], ALU.mult, ALU.subtract),
            deps=[sq, mq])
        s2 = pr.op("s", lambda eng, sl=sl: eng.activation(
            sqv[sl][:], varo[sl][:], AF.Sqrt, bias=epsb[:], scale=1.0),
            deps=[vr, o_eps])
        rt = pr.op("v", lambda eng, sl=sl: eng.reciprocal(rstd[sl][:], sqv[sl][:]),
                   deps=[s2], fence=True)
        xn = pr.op("v", lambda eng, sl=sl: eng.tensor_scalar(
            xn_sb[0][:], x_sb[sl][:], nmean[sl][:], rstd[sl][:], ALU.add, ALU.mult),
            deps=[rt, sq])
        x_consumer[sl] = xn
        if dst_scr is None:
            return xn
        wr = pr.op("s", lambda eng, dst_row=dst_row: eng.dma_start(
            out=dst_scr[dst_row:dst_row + 128, :], in_=xn_sb[0][:]),
            deps=[xn], chan="xnw0")
        xn_writer[0] = wr
        return wr

    x_sb = [salloc(f"x_sb{i}", [128, D], F32, side="right") for i in range(2)]
    xn_sb = [salloc("xn_sb0", [128, D], BF16, side="right")]
    wkv_res = salloc("wkv_res", [128, NF * 1024], BF16, side="right")
    wkv32 = [salloc(f"wkv32_{i}", [128, 4096], F32) for i in range(2)]
    wkv_gate = []
    _wkv_cvt = [None, None]
    for blk in range(8):
        sl = blk % 2
        ld = pr.op("q", lambda eng, blk=blk, sl=sl: eng.dma_start(
            out=wkv32[sl][:].rearrange("p (a c) -> p a c", c=1024),
            in_=wkv_c[blk * 512:(blk + 1) * 512, :].rearrange(
                "(a p) c -> p a c", p=128)),
            deps=[_wkv_cvt[sl]], chan=f"wr{sl}")
        cv = pr.op("v", lambda eng, blk=blk, sl=sl: eng.tensor_copy(
            wkv_res[:, blk * 4096:(blk + 1) * 4096], wkv32[sl][:]), deps=[ld])
        _wkv_cvt[sl] = cv
        wkv_gate = [cv] if blk == 7 else wkv_gate + ([cv] if blk == 6 else [])
    sfree("wkv32_1", "wkv32_0")
    xnT_own = salloc("xnT_own", [128, 2 * NF * 512], BF16)
    xnT_tmp = salloc("xnT_tmp", [128, NF * 512], BF16)
    kst = [salloc(f"kst{i}", [128, 512], BF16) for i in range(2)]
    vfm = [salloc(f"vfm{i}", [128, 512], BF16, side="right") for i in range(2)]
    vst = [salloc(f"vst{i}", [128, 512], BF16, side="right") for i in range(2)]
    kst_user = [None] * 2
    vfm_user = [None] * 2
    vst_user = [None] * 2
    kv_write_ops = []

    def xnT_tile(ck, d):
        if ck < 2:
            return xnT_own[:, (ck * NF + d) * 512:(ck * NF + d + 1) * 512]
        return xnT_tmp[:, d * 512:(d + 1) * 512]

    xnT_view_own = xnT_own[:].rearrange("p (b f c) -> p b f c", b=2, c=512)
    xnT_view_tmp = xnT_tmp[:].rearrange("p (f c) -> p f c", c=512)

    def emit_ln(ci, st):
        base = ci * 512
        return ln_tile((lambda r0=base + st * 128: x_kv[r0:r0 + 128, :]), st, None, 0)

    region_user = [[None] * 4, [None] * 4, [None] * 4]

    def emit_tr_subtile(ck, st, xn, ready):
        rbuf = min(ck, 2)
        for f4 in range(8):
            b = 4 + f4 % 2
            mms = []
            for fi in range(4):
                f = f4 * 4 + fi
                dd = [xn, c_ident]
                if fi == 0:
                    dd.append(bank_user.get(b))
                mms.append(pr.op("t", lambda eng, f=f, fi=fi, b=b: eng.matmul(
                    psA[b][:, fi * 128:(fi + 1) * 128],
                    xn_sb[0][:, f * 128:(f + 1) * 128], ident[:],
                    start=True, stop=True), dd))
            if ck < 2:
                out_ap = xnT_view_own[:, ck, f4 * 4:(f4 + 1) * 4,
                                      st * 128:(st + 1) * 128]
            else:
                out_ap = xnT_view_tmp[:, f4 * 4:(f4 + 1) * 4, st * 128:(st + 1) * 128]
            ev = pr.op("v", lambda eng, b=b, out_ap=out_ap: eng.tensor_copy(
                out_ap, psA[b][:].rearrange("p (a c) -> p a c", c=128)),
                deps=mms + [region_user[rbuf][f4 // 2]])
            consume_bank(b, ev)
            for fi in range(4):
                ready[f4 * 4 + fi] = ev
        xn_reader[0] = mms[-1]

    ready_by_ck = [[None] * NF for _ in range(4)]
    for st in range(4):
        emit_tr_subtile(0, st, emit_ln(0, st), ready_by_ck[0])

    for ck in range(4):
        xnT_ready = ready_by_ck[ck]
        ready_next = ready_by_ck[ck + 1] if ck + 1 < 4 else None
        ln_next = [emit_ln(ck + 1, 0)] if ck + 1 < 3 else []
        seg = 0
        for w in range(2):
            lasts = [None] * 4
            for dblk in range(2):
                for d in range(dblk * 16, dblk * 16 + 16):
                    for gi in range(4):
                        stf = d == 0
                        spf = d == NF - 1
                        dd = wkv_gate + [xnT_ready[d]]
                        if stf:
                            dd.append(bank_user.get(gi))
                        lasts[gi] = pr.op("t", (lambda eng, gi=gi, w=w, d=d, ck=ck,
                                                stf=stf, spf=spf:
                                                eng.matmul(psA[gi][:],
                                                           wkv_res[:, d * 1024 + w * 512
                                                                   + gi * 128:
                                                                   d * 1024 + w * 512
                                                                   + (gi + 1) * 128],
                                                           xnT_tile(ck, d),
                                                           start=stf, stop=spf)), dd)
                    if w == 1 and d % 8 == 7:
                        region_user[min(ck, 2)][(d // 8) % 4] = lasts[3]
                if ck + 1 < 3 and seg < 4:
                    emit_tr_subtile(ck + 1, seg, ln_next[seg], ready_next)
                    if seg + 1 < 4:
                        ln_next.append(emit_ln(ck + 1, seg + 1))
                    seg += 1
            if w == 0:
                for s in range(4):
                    sb2 = s % 2
                    ev = pr.op("v", lambda eng, s=s, sb2=sb2: eng.tensor_copy(
                        kst[sb2][:], psA[s][:]), deps=[lasts[s], kst_user[sb2]])
                    consume_bank(s, ev)
                    wr = pr.op("s", lambda eng, s=s, sb2=sb2, ck=ck: eng.dma_start(
                        out=k_src[s * 128:(s + 1) * 128, ck * 512:(ck + 1) * 512],
                        in_=kst[sb2][:]), deps=[ev], chan=f"ks{s % 2}")
                    kst_user[sb2] = wr
                    kv_write_ops.append(wr)
            else:
                for s in range(4):
                    sb2 = s % 2
                    ev = pr.op("v", lambda eng, s=s, sb2=sb2: eng.tensor_copy(
                        vfm[sb2][:], psA[s][:]), deps=[lasts[s], vfm_user[sb2]])
                    consume_bank(s, ev)
                    b = 6 + s % 2
                    tmms = []
                    for fi in range(4):
                        dd = [ev, c_ident]
                        if fi == 0:
                            dd.append(bank_user.get(b))
                        tmms.append(pr.op("t", lambda eng, sb2=sb2, fi=fi, b=b: eng.matmul(
                            psA[b][:, fi * 128:(fi + 1) * 128],
                            vfm[sb2][:, fi * 128:(fi + 1) * 128], ident[:],
                            start=True, stop=True), dd))
                    cp = pr.op("s", lambda eng, sb2=sb2, b=b: eng.copy(
                        vst[sb2][:], psA[b][:]), deps=tmms + [vst_user[sb2]])
                    consume_bank(b, cp)
                    vfm_user[sb2] = tmms[-1]
                    wr = pr.op("s", lambda eng, s=s, sb2=sb2, ck=ck: eng.dma_start(
                        out=v_src[ck * 512:(ck + 1) * 512, s * 128:(s + 1) * 128]
                            .rearrange("(a p) c -> p a c", p=128),
                        in_=vst[sb2][:].rearrange("p (a c) -> p a c", c=128)),
                        deps=[cp], chan=f"vs{s % 2}")
                    vst_user[sb2] = wr
                    kv_write_ops.append(wr)

        if ck == 2:
            for st in range(4):
                xn3 = emit_ln(3, st)
                emit_tr_subtile(3, st, xn3, ready_by_ck[3])

    own_ready = [[ready_by_ck[0][d], ready_by_ck[1][d]] for d in range(NF)]
    kv_lasts = list({o.chan: o for o in kv_write_ops}.values())
    RG = [[0, 1, 2, 3], [4, 5, 6, 7]]
    ag_k = pr.op("g", lambda eng: eng.collective_compute(
        "AllGather", ALU.bypass, replica_groups=RG, ins=[k_src], outs=[k_dst]),
        deps=kv_lasts)
    ag_v = pr.op("g", lambda eng: eng.collective_compute(
        "AllGather", ALU.bypass, replica_groups=RG, ins=[v_src], outs=[v_dst]),
        deps=kv_lasts)

    sfree("kst1", "kst0", "xnT_tmp")
    sfree("vst1", "vst0", "vfm1", "vfm0", "wkv_res", "xn_sb0", "x_sb1", "x_sb0")

    wtile32 = [salloc(f"wt32_{i}", [128, 2048], F32, side="right") for i in range(4)]
    wtileu8 = [salloc(f"wtu8_{i}", [128, 2048], U8, side="right") for i in range(4)]
    pmt = [salloc(f"pmt{i}", [128, 2048], BF16, side="right") for i in range(4)]

    u8_user = [None] * 4
    wtile_user = [None] * 4

    def proj_sweep(w_dram, in_slice, in_ready, consumer, pm_u8=False, gate=None,
                   ld_gate=()):
        for w in range(8):
            lasts = [None] * 8
            for d in range(NF):
                s = (d // 4) % 4
                di = d % 4
                if di == 0:
                    raw = wtileu8[s] if pm_u8 else wtile32[s]
                    wt = pr.op("q", lambda eng, w=w, d=d, raw=raw, wd=w_dram: eng.dma_start(
                        out=raw[:].rearrange("p (a c) -> p a c", c=512),
                        in_=wd[w, d * 128:(d + 4) * 128, :].rearrange(
                            "(a p) c -> p a c", p=128)),
                        deps=[u8_user[s]] + list(ld_gate), chan=f"w{s}")
                    if pm_u8:
                        pm = pr.op("v", lambda eng, s=s, raw=raw: eng.tensor_scalar(
                            pmt[s][:], raw[:], 2.0, -1.0, ALU.mult, ALU.add),
                            deps=[wt, wtile_user[s]])
                    else:
                        pm = pr.op("v", lambda eng, s=s, raw=raw: eng.tensor_copy(
                            pmt[s][:], raw[:]), deps=[wt, wtile_user[s]])
                    u8_user[s] = pm
                op_src = pmt[s]
                dep0 = pm
                for fi in range(4):
                    for tt in range(2):
                        slot = fi * 2 + tt
                        stf = d == 0
                        spf = d == NF - 1
                        dd = [dep0] + list(in_ready[d])
                        if stf:
                            dd.append(bank_user.get(slot))
                            if gate is not None:
                                dd.append(gate)
                        rhs_ap = in_slice(d, tt)
                        lasts[slot] = pr.op("t", (lambda eng, slot=slot, src=op_src, di=di,
                                                  fi=fi, stf=stf, spf=spf, rhs=rhs_ap:
                                                  eng.matmul(psA[slot][:],
                                                             src[:, di * 512 + fi * 128:
                                                                 di * 512 + (fi + 1) * 128],
                                                             rhs, start=stf, stop=spf)), dd)
                if di == 3:
                    wtile_user[s] = lasts[7]
            for fi in range(4):
                for tt in range(2):
                    slot = fi * 2 + tt
                    cop = consumer(w * 4 + fi, tt, slot, lasts[slot])
                    consume_bank(slot, cop)

    qstage = [salloc(f"qstage{i}", [128, OWN], BF16, side="right") for i in range(2)]
    qstage_user = [None, None]
    qscr_writes = []
    q_evs = {}

    def q_consume(f, tt, slot, mmop):
        sl = f % 2
        ev = pr.op("v", lambda eng, sl=sl, tt=tt, slot=slot: eng.tensor_copy(
            qstage[sl][:, tt * 512:(tt + 1) * 512], psA[slot][:]),
            deps=[mmop, qstage_user[sl]] + kv_lasts)
        q_evs.setdefault(f, []).append(ev)
        if tt == 1:
            wr = pr.op("s", lambda eng, f=f, sl=sl: eng.dma_start(
                out=qscr[f], in_=qstage[sl][:]), deps=q_evs[f], chan=f"qs{sl}")
            qstage_user[sl] = wr
            qscr_writes.append(wr)
        return ev

    own_slice = lambda d, tt: xnT_own[:, (tt * NF + d) * 512:(tt * NF + d + 1) * 512]
    proj_sweep(wq_t, own_slice, own_ready, q_consume, ld_gate=kv_lasts)
    q_lasts = list({o.chan: o for o in qscr_writes}.values())

    relu_sb = [salloc(f"relu{i}", [128, 512], F32, side="right") for i in range(4)]
    relu_user = [None] * 4
    h_a = [salloc(f"h_a{f}", [128, OWN], BF16, side="right") for f in range(NF)]
    h_ready = [None] * NF

    def h_ready_l():
        return [[h_ready[d]] for d in range(NF)]

    def h2_ready_l():
        return [[h2_ready[d]] for d in range(NF)]

    def ffn_consume_factory(out_tiles, out_ready, s_col):
        def consume(f, tt, slot, mmop):
            rsl = slot % 4
            rl = pr.op("s", lambda eng, slot=slot, rsl=rsl: eng.activation(
                relu_sb[rsl][:], psA[slot][:], AF.Relu, scale=sA[:, s_col:s_col + 1]),
                deps=[mmop, relu_user[rsl], c_sA] + kv_lasts)
            mu = pr.op("v", lambda eng, f=f, tt=tt, rsl=rsl: eng.tensor_scalar_mul(
                out_tiles[f][:, tt * 512:(tt + 1) * 512], relu_sb[rsl][:],
                eisc_sb[:, f:f + 1]), deps=[rl, c_eisc])
            relu_user[rsl] = mu
            out_ready[f] = mu
            return rl
        return consume

    proj_sweep(w0_t, own_slice, own_ready, ffn_consume_factory(h_a, h_ready, 0), pm_u8=True)
    sfree("xnT_own")
    h_b = [salloc(f"h_b{f}", [128, OWN], BF16) for f in range(NF)]
    h2_ready = [None] * NF
    proj_sweep(w1_t, lambda d, tt: h_a[d][:, tt * 512:(tt + 1) * 512],
               [[h_ready[d]] for d in range(NF)] if False else h_ready_l(),
               ffn_consume_factory(h_b, h2_ready, 1), pm_u8=True)
    sfree(*(f"h_a{f}" for f in reversed(range(NF))))

    gstage = [salloc(f"gstage{i}", [128, OWN], BF16, side="right") for i in range(2)]
    gstage_user = [None, None]
    gscr_writes = []
    g_evs = {}

    def g_consume(f, tt, slot, mmop):
        sl = f % 2
        ev = pr.op("s", lambda eng, sl=sl, tt=tt, slot=slot: eng.activation(
            gstage[sl][:, tt * 512:(tt + 1) * 512], psA[slot][:], AF.Silu,
            scale=sA[:, 2:3]), deps=[mmop, gstage_user[sl], c_sA])
        g_evs.setdefault(f, []).append(ev)
        if tt == 1:
            wr = pr.op("s", lambda eng, f=f, sl=sl: eng.dma_start(
                out=gscr[f], in_=gstage[sl][:]), deps=g_evs[f], chan=f"gs{sl}")
            gstage_user[sl] = wr
            gscr_writes.append(wr)
        return ev

    proj_sweep(w2_t, lambda d, tt: h_b[d][:, tt * 512:(tt + 1) * 512],
               h2_ready_l(), g_consume, pm_u8=True)
    g_lasts = list({o.chan: o for o in gscr_writes}.values())
    sfree(*(f"h_b{f}" for f in reversed(range(NF))))
    sfree("gstage1", "gstage0",
          *(f"relu{i}" for i in reversed(range(4))),

          "qstage1", "qstage0")

    pfree(*(f"psA{i}" for i in reversed(range(8))))
    psSC = [palloc(f"psSC{i}", [128, 1024]) for i in range(2)]
    psCTX = [palloc(f"psCTX{i}", [128, 512]) for i in range(2)]
    psDEN = palloc("psDEN", [128, 512])
    psBC = palloc("psBC", [128, 512])

    ctx_sb = [salloc(f"ctx{f}", [128, OWN], BF16) for f in range(NF)]
    kbuf = [salloc(f"kbuf{i}", [128, T], BF16, side="right") for i in range(2)]
    vbuf = [salloc(f"vbuf{i}", [128, T], BF16, side="right") for i in range(2)]
    qhead = [salloc(f"qh{i}", [128, 512], BF16, side="right") for i in range(4)]
    probs = [salloc(f"probs{i}", [128, 1024], BF16, side="right") for i in range(3)]
    bc_sb = [salloc(f"bc_sb{i}", [128, 512], F32, side="right") for i in range(2)]
    pacc = [salloc(f"pacc{i}", [128, 512], BF16, side="right") for i in range(2)]
    rrow = [salloc(f"rrow{i}", [1, 512], F32, side="right") for i in range(2)]
    ones_cb = salloc("ones_cb", [128, 1], BF16, side="right")
    o_onescb = pr.op("v", lambda eng: eng.memset(ones_cb[:], 1.0))
    msk4 = salloc("msk4", [128, 4 * 128], BF16, side="right")
    c_msk = pr.op("q", lambda eng: eng.dma_start(
        out=msk4[:].rearrange("p (a c) -> p a c", c=128),
        in_=msk4_t.rearrange("a p c -> p a c")), chan="c1")

    kv_user = [None, None]
    qhead_user = [None] * 4
    probs_user = [[], [], []]
    pacc_user = [None, None]
    ctx_ready = [None] * NF
    pending_tail = None

    for qt in range(2):
        nkc = (qt + 1) * 16
        ng = nkc // 2
        for gg in range(G):
            sl = gg % 2
            gl = gg // 2
            b_in = gg % 2
            E4 = (qt + 1) * 512
            ldk_ops = []
            ldv_ops = []
            for r4 in range(4):
                ldk_ops.append(pr.op("q", lambda eng, r4=r4, gl=gl, b_in=b_in, sl=sl,
                                     E4=E4: eng.dma_start(
                    out=kbuf[sl][:, r4 * E4:(r4 + 1) * E4],
                    in_=k_dst[r4 * 512 + gl * 128:r4 * 512 + (gl + 1) * 128,
                              b_in * OWN:b_in * OWN + E4]),
                    deps=[ag_k] + g_lasts + [kv_user[sl]], chan=f"kb{sl}"))
                ldv_ops.append(pr.op("q", lambda eng, r4=r4, gl=gl, b_in=b_in, sl=sl,
                                     E4=E4: eng.dma_start(
                    out=vbuf[sl][:, r4 * E4:(r4 + 1) * E4].rearrange(
                        "p (a c) -> p a c", c=128),
                    in_=v_dst[r4 * 2 * OWN + b_in * OWN:
                              r4 * 2 * OWN + b_in * OWN + E4, gl * 128:(gl + 1) * 128]
                        .rearrange("(a p) c -> p a c", p=128)),
                    deps=[ag_v] + g_lasts + [kv_user[sl]], chan=f"vb{sl}"))
            ldk = ldk_ops[-1]
            ldv = ldv_ops[-1]
            for h in range(HPG):
                i = gg * HPG + h
                qsl = i % 4
                ldq = pr.op("q", lambda eng, i=i, qt=qt, qsl=qsl: eng.dma_start(
                    out=qhead[qsl][:], in_=qscr[i][:, qt * 512:(qt + 1) * 512]),
                    deps=q_lasts + g_lasts + [qhead_user[qsl]], chan=f"qh{qsl}")
                ctx_bank = 4 + h % 2

                def emit_scores(kg, ldk=ldk, ldq=ldq, sl=sl, qsl=qsl, qt=qt,
                                nkc=nkc):
                    sb = kg % 2
                    npr = (qt + 1) * 4
                    mms = []
                    for ci in range(2):
                        kc = kg * 2 + ci
                        lp = kc % npr
                        db = max(0, lp - qt * 4)
                        q0 = 128 * db
                        bank = sb * 2 + ci
                        mms.append(pr.op("t", lambda eng, sb=sb, ci=ci, kc=kc, sl=sl,
                                         qsl=qsl, q0=q0: eng.matmul(
                            psSC[sb][:, ci * 512 + q0:(ci + 1) * 512],
                            kbuf[sl][:, kc * 128:(kc + 1) * 128], qhead[qsl][:, q0:512],
                            start=True, stop=True),
                            deps=[ldk, ldq, bank_user.get(bank)]))
                    return mms

                sc_q = [emit_scores(0)]
                if ng > 1:
                    sc_q.append(emit_scores(1))
                last_den = None
                last_ctx = None
                for kg in range(ng):
                    mms = sc_q.pop(0)
                    if kg == 0 and pending_tail is not None:
                        pending_tail()
                        pending_tail = None
                    sb = kg % 2
                    psl = kg % 3
                    ex = pr.op("s", lambda eng, sb=sb, psl=psl: eng.activation(
                        probs[psl][:], psSC[sb][:], AF.Exp, bias=eshift[:],
                        scale=ATT_SCALE), deps=mms + [o_esh] + probs_user[psl])
                    consume_bank(sb * 2, ex)
                    consume_bank(sb * 2 + 1, ex)
                    if kg + 2 < ng:
                        sc_q.append(emit_scores(kg + 2))
                    vops_ci = [[], []]
                    npr = (qt + 1) * 4
                    for ci in range(2):
                        kc = kg * 2 + ci
                        r4 = kc // npr
                        lp = kc % npr
                        dband = lp - qt * 4
                        cb = ci * 512
                        if dband >= 0:
                            if dband > 0:
                                vops_ci[ci].append(pr.op("v", lambda eng, psl=psl, cb=cb,
                                                         dband=dband: eng.memset(
                                    probs[psl][:, cb:cb + 128 * dband], 0.0), deps=[ex]))
                            vops_ci[ci].append(pr.op("v", lambda eng, psl=psl, cb=cb,
                                                     dband=dband, r4=r4: eng.tensor_tensor(
                                probs[psl][:, cb + 128 * dband:cb + 128 * dband + 128],
                                probs[psl][:, cb + 128 * dband:cb + 128 * dband + 128],
                                msk4[:, r4 * 128:(r4 + 1) * 128], ALU.mult),
                                deps=[ex, c_msk]))
                    pp = pr.op("v", lambda eng, psl=psl, kg=kg: eng.tensor_tensor(
                        pacc[kg % 2][:], probs[psl][:, 0:512], probs[psl][:, 512:1024],
                        ALU.add), deps=[ex] + vops_ci[0] + vops_ci[1] +
                        [pacc_user[kg % 2]], fence=True)
                    pacc_user[kg % 2] = None
                    dd = [pp, o_onescb]
                    if kg == 0:
                        dd.append(bank_user.get(6))
                    last_den = pr.op("t", (lambda eng, kg=kg, stf=(kg == 0),
                                           spf=(kg == ng - 1):
                                           eng.matmul(psDEN[0:1, :], ones_cb[:],
                                                      pacc[kg % 2][:],
                                                      start=stf, stop=spf)), dd)
                    pacc_user[kg % 2] = last_den
                    for ci in range(2):
                        kc = kg * 2 + ci
                        lp = kc % npr
                        db = max(0, lp - qt * 4)
                        q0 = 128 * db
                        cb = ci * 512
                        dd = [ldv, ex] + vops_ci[ci]
                        if kc == 0:
                            dd.append(bank_user.get(ctx_bank))
                        last_ctx = pr.op("t", (lambda eng, psl=psl, cb=cb, kc=kc, sl=sl,
                                               cbk=ctx_bank, q0=q0, stf=(kc == 0),
                                               spf=(kc == nkc - 1):
                                               eng.matmul(psCTX[cbk - 4][:, q0:512],
                                                          vbuf[sl][:, kc * 128:(kc + 1) * 128],
                                                          probs[psl][:, cb + q0:cb + 512],
                                                          start=stf, stop=spf)), dd)
                    probs_user[psl] = [last_ctx]
                qhead_user[qsl] = last_ctx

                def tail(i=i, qt=qt, cbk=ctx_bank, last_den=last_den,
                         last_ctx=last_ctx):
                    par = (i * 2 + qt) % 2
                    rc = pr.op("v", lambda eng, par=par: eng.reciprocal(
                        rrow[par][:], psDEN[0:1, :]), deps=[last_den], fence=True)
                    consume_bank(6, rc)
                    bc_mm = pr.op("t", lambda eng, par=par: eng.matmul(
                        psBC[:], ones_r[:], rrow[par][:], start=True, stop=True),
                        deps=[rc, o_onesr, bank_user.get(7)])
                    bcc = pr.op("v", lambda eng, par=par: eng.tensor_copy(
                        bc_sb[par][:], psBC[:]), deps=[bc_mm], fence=True)
                    consume_bank(7, bcc)
                    nrm = pr.op("v", lambda eng, i=i, qt=qt, par=par, cbk=cbk:
                                eng.tensor_tensor(
                        ctx_sb[i][:, qt * 512:(qt + 1) * 512], psCTX[cbk - 4][:],
                        bc_sb[par][:], ALU.mult), deps=[last_ctx, bcc])
                    consume_bank(cbk, nrm)
                    ctx_ready[i] = [nrm]
                    return nrm

                pending_tail = tail
            kv_user[sl] = qhead_user[(gg * HPG + HPG - 1) % 4]

    if pending_tail is not None:
        pending_tail()
        pending_tail = None
    d_done = ctx_ready[H - 1][0]
    if ctxscr is not None:
        for f in range(NF):
            pr.op("q", lambda eng, f=f: eng.dma_start(out=ctxscr[f], in_=ctx_sb[f][:]),
                  deps=ctx_ready[f], chan="dbgx")
    sfree("msk4", "ones_cb", "rrow1", "rrow0", "pacc1", "pacc0", "bc_sb1", "bc_sb0",
          "probs2", "probs1", "probs0",
          "qh3", "qh2", "qh1", "qh0", "vbuf1", "vbuf0", "kbuf1", "kbuf0")
    pfree("psBC", "psDEN", "psCTX1", "psCTX0", "psSC1", "psSC0")
    psA = [palloc(f"psE{i}", [128, 512]) for i in range(8)]

    sg = [salloc(f"sg{f}", [128, OWN], BF16, side="right") for f in range(NF)]
    s_sb = [salloc(f"s_sb{i}", [128, 512], BF16, side="right") for i in range(2)]
    gtile = [salloc(f"gtile{i}", [128, OWN], BF16, side="right") for i in range(4)]
    s_user = [None, None]
    gtile_user = [None] * 4
    gld = {}
    sg_ready = [None] * NF

    def prefetch_g(f):
        if f not in gld:
            gld[f] = pr.op("q", lambda eng, f=f: eng.dma_start(
                out=gtile[f % 4][:], in_=gscr[f]),
                deps=g_lasts + [d_done, gtile_user[f % 4]], chan=f"gt{f % 4}")
        return gld[f]

    def wo_consume(f, tt, slot, mmop):
        ssl = slot % 2
        ev = pr.op("s", lambda eng, slot=slot, ssl=ssl, f=f: eng.activation(
            s_sb[ssl][:], psA[slot][:], AF.Identity, bias=bo_sb[:, f:f + 1], scale=1.0),
            deps=[mmop, s_user[ssl], c_bo])
        gop = prefetch_g(f)
        mu = pr.op("v", lambda eng, f=f, tt=tt, ssl=ssl: eng.tensor_tensor(
            sg[f][:, tt * 512:(tt + 1) * 512], s_sb[ssl][:],
            gtile[f % 4][:, tt * 512:(tt + 1) * 512], ALU.mult), deps=[ev, gop])
        s_user[ssl] = mu
        gtile_user[f % 4] = mu
        sg_ready[f] = mu
        return ev

    proj_sweep(wo_t, lambda d, tt: ctx_sb[d][:, tt * 512:(tt + 1) * 512],
               ctx_ready, wo_consume, gate=d_done)
    sfree("gtile3", "gtile2", "gtile1", "gtile0", "s_sb1", "s_sb0")
    sfree(*(f"ctx{f}" for f in reversed(range(NF))))

    if sgscr is not None:
        for f in range(NF):
            pr.op("q", lambda eng, f=f: eng.dma_start(out=sgscr[f], in_=sg[f][:]),
                  deps=[sg_ready[f]], chan="dbgx")
    xrow = [salloc(f"xrow{i}", [128, D], F32) for i in range(3)]
    orow = [salloc(f"orow{i}", [128, D], F32) for i in range(2)]
    xrow_user = [None] * 3
    orow_user = [None] * 2
    F_BANKS = (4, 5)
    e_gate = sg_ready[NF - 1]
    for ts in range(8):
        sl = ts % 3
        so = ts % 2
        ldx = pr.op("q", lambda eng, ts=ts, sl=sl: eng.dma_start(
            out=xrow[sl][:], in_=x_kv[ts * 128:(ts + 1) * 128, :]),
            deps=[xrow_user[sl], e_gate], chan=f"xr{sl}")
        adds = []
        for f4 in range(8):
            b = F_BANKS[f4 % 2]
            mms = []
            for fi in range(4):
                f = f4 * 4 + fi
                dd = [sg_ready[f], c_ident]
                if fi == 0:
                    dd.append(bank_user.get(b))
                mms.append(pr.op("t", lambda eng, b=b, f=f, fi=fi, ts=ts: eng.matmul(
                    psA[b][:, fi * 128:(fi + 1) * 128], sg[f][:, ts * 128:(ts + 1) * 128],
                    ident[:], start=True, stop=True), dd))
            ad = pr.op("v", lambda eng, b=b, f4=f4, sl=sl, so=so: eng.tensor_tensor(
                orow[so][:, f4 * 512:(f4 + 1) * 512], xrow[sl][:, f4 * 512:(f4 + 1) * 512],
                psA[b][:], ALU.add), deps=mms + [ldx, orow_user[so]])
            consume_bank(b, ad)
            adds.append(ad)
        xrow_user[sl] = adds[-1]
        wr = pr.op("q", lambda eng, ts=ts, so=so: eng.dma_start(
            out=out_t[ts * 128:(ts + 1) * 128, :], in_=orow[so][:]), deps=adds,
            chan=f"or{so}")
        orow_user[so] = wr

    pr.finalize()
    with nc.Block() as block:
        @block.tensor
        def _(eng):
            pr.run_engine("t", eng, sems)

        @block.vector
        def _(eng):
            pr.run_engine("v", eng, sems)

        @block.scalar
        def _(eng):
            pr.run_engine("s", eng, sems)

        @block.sync
        def _(eng):
            pr.run_engine("q", eng, sems)

        @block.gpsimd
        def _(eng):
            pr.run_engine("g", eng, sems)

    for name in reversed(list(allocs)):
        sfree(name)
    for cm in reversed(list(psums.values())):
        cm.__exit__(None, None, None)
    for cm in reversed(list(sem_cms.values())):
        cm.__exit__(None, None, None)
    return nc


def _host_inputs(L0_attn_x, L0_attn_Wq, L0_attn_Wkv, L0_attn_Wo, L0_attn_bo,
                 L0_ffn_k0, L0_ffn_s0, L0_ffn_k1, L0_ffn_s1, L0_ffn_k2, L0_ffn_s2):
    x = np.ascontiguousarray(np.asarray(L0_attn_x, dtype=np.float32))
    Wq = np.asarray(L0_attn_Wq, dtype=np.float32)
    Wkv = np.asarray(L0_attn_Wkv, dtype=np.float32)
    Wo = np.asarray(L0_attn_Wo, dtype=np.float32)
    bo = np.asarray(L0_attn_bo, dtype=np.float32)

    kv4 = Wkv.reshape(D, G, 2, HD)
    wkv_by_b = []
    for b in range(BATCH):
        kc = kv4[:, 4 * b:4 * b + 4, 0, :].reshape(D, 512)
        vc = kv4[:, 4 * b:4 * b + 4, 1, :].reshape(D, 512)
        wkv_by_b.append(np.ascontiguousarray(np.concatenate([kc, vc], axis=1)))
    wq_t = np.ascontiguousarray(Wq.reshape(D, 8, 512).transpose(1, 0, 2))
    wo_t = np.ascontiguousarray(Wo.reshape(D, 8, 512).transpose(1, 0, 2))

    def tile_u8(k):
        a = np.asarray(k)
        a = a.view(np.uint8) if a.dtype == np.bool_ else a.astype(np.uint8)
        return np.ascontiguousarray(a.reshape(D, 8, 512).transpose(1, 0, 2))

    w0_t = tile_u8(L0_ffn_k0)
    w1_t = tile_u8(L0_ffn_k1)
    w2_t = tile_u8(L0_ffn_k2)

    bo_t = np.ascontiguousarray(bo.reshape(NF, 128).T)
    eisc = np.where(np.arange(D) < NE, np.float32(2.0), np.float32(2.0 * EI)).astype(np.float32)
    eisc_t = np.ascontiguousarray(eisc.reshape(NF, 128).T)
    sA_col = np.stack([np.float32(np.asarray(L0_ffn_s0)),
                       np.float32(np.asarray(L0_ffn_s1)),
                       np.float32(np.asarray(L0_ffn_s2)),
                       np.float32(0.0)])
    sA_t = np.ascontiguousarray(np.broadcast_to(sA_col[None, :], (128, 4))).astype(np.float32)
    ident_t = np.eye(128, dtype=BF)

    shared = dict(w0_t=w0_t, w1_t=w1_t,
                  w2_t=w2_t, bo_t=bo_t, eisc_t=eisc_t, sA_t=sA_t, ident_t=ident_t)
    perm = [1, 0, 3, 2, 5, 4, 7, 6]
    wq_t_b = [wq_t, np.ascontiguousarray(wq_t[perm])]
    Wo_p = np.ascontiguousarray(Wo.reshape(8, 512, D)[perm].reshape(D, D))
    wo_t_b = [wo_t, np.ascontiguousarray(Wo_p.reshape(D, 8, 512).transpose(1, 0, 2))]
    in_maps = []
    ki = np.arange(128)[:, None]
    wi = np.arange(128)[None, :]
    for c in range(8):
        b, j = divmod(c, 4)
        msk4 = np.stack([(ki <= wi - (1 if r4 > j else 0)) for r4 in range(4)])
        m = dict(shared)
        m["x_kv"] = np.ascontiguousarray(
            np.concatenate([x[b, j::4, :], x[1 - b, j::4, :]], axis=0))
        m["msk4_t"] = msk4.astype(BF)
        m["wkv_c"] = wkv_by_b[b]
        m["wq_t"] = wq_t_b[b]
        m["wo_t"] = wo_t_b[b]
        in_maps.append(m)
    return x, in_maps

def kernel(L0_attn_x, L0_attn_Wq, L0_attn_Wkv, L0_attn_Wo, L0_attn_bo,
           L0_ffn_k0, L0_ffn_s0, L0_ffn_k1, L0_ffn_s1, L0_ffn_k2, L0_ffn_s2):
    x, in_maps = _host_inputs(L0_attn_x, L0_attn_Wq, L0_attn_Wkv, L0_attn_Wo,
                              L0_attn_bo, L0_ffn_k0, L0_ffn_s0, L0_ffn_k1,
                              L0_ffn_s1, L0_ffn_k2, L0_ffn_s2)
    nc = build_program()
    res = run_bass_kernel_spmd(nc, in_maps, core_ids=list(range(8)))
    out = np.empty((BATCH, T, D), dtype=np.float32)
    for c in range(8):
        b, j = divmod(c, 4)
        out[b, j::4, :] = res.results[c]["out"]
    return out
